# version 02
# baseline (speedup 1.0000x reference)
import math
from contextlib import ExitStack

import numpy as np
import concourse.bass as bass
import concourse.mybir as mybir
from concourse.bass_utils import run_bass_kernel_spmd

F32 = mybir.dt.float32
BF16 = mybir.dt.bfloat16
AF = mybir.ActivationFunctionType
ALU = mybir.AluOpType
AX = mybir.AxisListType.X

H = 2048
I = 8192
HT = H // 128
EPS = 1e-5
ISCALE = 1.0 / math.sqrt(2048.0)
EXP_MARGIN = 2.0


def _skippable(key):
    if key is None:
        return True
    if isinstance(key, tuple):
        return any(isinstance(c, int) and c < 0 for c in key)
    return False


class K:
    def __init__(self, nc, ctx):
        self.nc = nc
        self.ctx = ctx
        self._sems = {}
        self._sbufs = {}
        self.tk = {}

    def sem(self, name):
        if name not in self._sems:
            self._sems[name] = [self.ctx.enter_context(self.nc.semaphore(name)), 0]
        return self._sems[name]

    def done(self, sname, inst, n=1):
        s = self.sem(sname)
        inst.then_inc(s[0], n)
        s[1] += n
        return (s[0], s[1])

    def put(self, key, ticket):
        self.tk[key] = ticket

    def waits(self, eng, *keys):
        best = {}
        for key in keys:
            if _skippable(key):
                continue
            t = self.tk.get(key)
            assert t is not None, f"missing ticket {key!r}"
            h, v = t
            if h not in best or best[h] < v:
                best[h] = v
        for h, v in best.items():
            eng.wait_ge(h, v)

    wait = waits

    def sbuf(self, name, shape, dtype):
        if name not in self._sbufs:
            self._sbufs[name] = self.ctx.enter_context(
                self.nc.sbuf_tensor(name, shape, dtype))
        return self._sbufs[name]

    def psum(self, name, shape, dtype=F32):
        return self.ctx.enter_context(self.nc.psum_tensor(name, shape, dtype))


def chain(k, eng, sname, inst):
    tk = k.done(sname, inst)
    eng.wait_ge(tk[0], tk[1])
    return tk


def wcol_ap(w, c0, ncols, t0=0, nt=None):
    r = w.rearrange("(t p) c -> t p c", p=128)
    nt = nt if nt is not None else r.shape[0] - t0
    return r[t0:t0 + nt, :, c0:c0 + ncols].transpose([1, 0, 2])


def emit_setup(k, block, ident, consts):
    @block.gpsimd
    def _(g):
        tk = k.done("setup", g.memset(ident[:, :], 0.0))
        g.memset(consts[:, 0:1], EPS)
        g.memset(consts[:, 1:2], -EXP_MARGIN)
        g.memset(consts[:, 2:3], 1.0)
        g.wait_ge(tk[0], tk[1])
        ins = g.affine_select(out=ident[:, :], in_=ident[:, :],
                              compare_op=ALU.not_equal, fill=1.0, base=0,
                              pattern=[[-1, 128]], channel_multiplier=1)
        k.put(("setup",), k.done("setup", ins))


def emit_ln(k, block, tag, x_dram, row0, n_tiles, yT, RSLAB, ident, ptr, eps_col,
            pred=None):
    xt = k.sbuf("ln.x", [128, 2 * H], F32)
    stt = k.sbuf("ln.st", [128, 8], F32)
    yt = k.sbuf("ln.y", [128, 2 * H], BF16)
    ptrb = ptr[:, :].bitcast(BF16)
    y3 = yT.rearrange("p (t c) -> p t c", c=RSLAB)

    for t in range(n_tiles):
        b = (t % 2) * H
        s0 = 4 * (t % 2)

        @block.sync
        def _(sync):
            if t == 0:
                k.wait(sync, pred)
            k.wait(sync, (tag, "yfin", t - 2))
            k.put((tag, "x", t), k.done(f"d.ln{t % 2}", sync.dma_start(
                out=xt[:, b:b + H],
                in_=x_dram[row0 + t * 128: row0 + (t + 1) * 128, :]), 16))

        @block.scalar
        def _(s):
            if t == 0:
                k.wait(s, pred)
            k.waits(s, (tag, "x", t), (tag, "yfin", t - 2))
            k.put((tag, "sumsq", t), k.done("s.ln", s.activation(
                yt[:, b:b + H], xt[:, b:b + H], AF.Square,
                accum_out=stt[:, s0 + 1:s0 + 2])))

        @block.vector
        def _(v):
            k.waits(v, (tag, "x", t), (tag, "std", t - 2))
            chain(k, v, "v.ln", v.reduce_sum(stt[:, s0:s0 + 1],
                                             xt[:, b:b + H], axis=AX))
            k.wait(v, (tag, "sumsq", t))
            chain(k, v, "v.ln", v.tensor_scalar(
                stt[:, s0 + 2:s0 + 3], stt[:, s0:s0 + 1],
                1.0 / H, None, ALU.mult))
            chain(k, v, "v.ln", v.tensor_scalar(
                stt[:, s0:s0 + 1], stt[:, s0 + 1:s0 + 2],
                1.0 / H, None, ALU.mult))
            chain(k, v, "v.ln", v.tensor_tensor(
                stt[:, s0 + 1:s0 + 2], stt[:, s0 + 2:s0 + 3],
                stt[:, s0 + 2:s0 + 3], ALU.mult))
            k.put((tag, "var", t), k.done("v.ln", v.tensor_tensor(
                stt[:, s0:s0 + 1], stt[:, s0:s0 + 1],
                stt[:, s0 + 1:s0 + 2], ALU.subtract)))

        @block.scalar
        def _(s):
            k.wait(s, (tag, "var", t))
            k.put((tag, "std", t), k.done("s.ln", s.activation(
                stt[:, s0 + 3:s0 + 4], stt[:, s0:s0 + 1], AF.Sqrt, bias=eps_col)))

        @block.vector
        def _(v):
            k.wait(v, (tag, "std", t))
            chain(k, v, "v.ln", v.reciprocal(stt[:, s0 + 1:s0 + 2],
                                             stt[:, s0 + 3:s0 + 4]))
            k.wait(v, (tag, "ytr", t - 2))
            tk = k.done("v.ln", v.tensor_scalar(
                yt[:, b:b + H], xt[:, b:b + H],
                stt[:, s0 + 2:s0 + 3], stt[:, s0 + 1:s0 + 2],
                ALU.subtract, ALU.mult))
            k.put((tag, "y", t), tk)
            k.put((tag, "yfin", t), tk)

        for hq in range(4):
            pb = (hq % 2) * 1024

            @block.tensor
            def _(t_):
                if t == 0 and hq == 0:
                    k.waits(t_, ("setup",), pred)
                k.waits(t_, (tag, "y", t), (tag, "evac", t * 4 + hq - 2))
                ins = None
                for j in range(4):
                    ht = hq * 4 + j
                    ins = t_.transpose(ptrb[:, pb + j * 128: pb + (j + 1) * 128],
                                       yt[:, b + ht * 128: b + (ht + 1) * 128],
                                       ident[:, :])
                tk = k.done("t.ln", ins)
                k.put((tag, "tr", t, hq), tk)
                if hq == 3:
                    k.put((tag, "ytr", t), tk)

            @block.vector
            def _(v):
                k.wait(v, (tag, "tr", t, hq))
                tk = k.done("v.ln", v.tensor_copy(
                    y3[:, hq * 4:(hq + 1) * 4, t * 128:(t + 1) * 128],
                    ptrb[:, pb:pb + 512].rearrange("p (j c) -> p j c", c=128)))
                k.put((tag, "evac", t * 4 + hq), tk)
                if hq == 3:
                    k.put((f"{tag}:yT", t), tk)


def emit_ffn_a(k, block, tag, ytag, wg, wu, yT, RSLAB, hs_dram, NTI,
               pg, pu, wbuf, band, pred=None):
    NIC_ = I // 512
    ICc = 512
    gbuf = k.sbuf("ffa.g", [128, 2 * ICc], F32)
    ubuf = k.sbuf("ffa.u", [128, 2 * ICc], F32)
    hbuf = k.sbuf("ffa.h", [128, 2 * ICc], F32)
    abuf = k.sbuf("ffa.ab", [128, 2 * ICc], F32)
    tbuf = k.sbuf("ffa.t", [128, 2 * ICc], F32)
    w4 = wbuf.rearrange("p (s t c) -> p s t c", s=3, c=ICc)

    for ic in range(NIC_):
        gslot, uslot = (2 * ic) % 3, (2 * ic + 1) % 3

        @block.gpsimd
        def _(g):
            for wi, w in ((0, wg), (1, wu)):
                n = 2 * ic + wi
                if n == 0:
                    k.wait(g, pred)
                if n >= 3:
                    pn = n - 3
                    key = "A.mmG" if pn % 2 == 0 else "A.mmU"
                    k.wait(g, (tag, key, (pn // 2) * NTI + NTI - 1))
                k.put((tag, "w", n), k.done(f"g.ffa{n % 3}", g.dma_start(
                    out=w4[:, n % 3], in_=wcol_ap(w, ic * ICc, ICc)), 16))

        for st in range(NTI):
            i = ic * NTI + st
            pb = (st % 2) * ICc
            sb = (i % 2) * ICc

            @block.tensor
            def _(t_):
                k.waits(t_, (tag, "w", 2 * ic), (tag, "w", 2 * ic + 1),
                        (f"{ytag}:yT", st),
                        (tag, "A.g", i - 2), (tag, "A.u", i - 2))
                ins = None
                for ht in range(HT):
                    ins = t_.matmul(pg[:, pb:pb + ICc],
                                    yT[:, ht * RSLAB + st * 128: ht * RSLAB + (st + 1) * 128],
                                    w4[:, gslot, ht],
                                    start=(ht == 0), stop=(ht == HT - 1))
                k.put((tag, "A.mmG", i), k.done("t.ffa", ins))
                for ht in range(HT):
                    ins = t_.matmul(pu[:, pb:pb + ICc],
                                    yT[:, ht * RSLAB + st * 128: ht * RSLAB + (st + 1) * 128],
                                    w4[:, uslot, ht],
                                    start=(ht == 0), stop=(ht == HT - 1))
                k.put((tag, "A.mmU", i), k.done("t.ffa", ins))

            @block.scalar
            def _(s):
                k.waits(s, (tag, "A.mmG", i), (tag, "A.V", i - 2))
                k.put((tag, "A.g", i), k.done("s.ffa", s.activation(
                    gbuf[:, sb:sb + ICc], pg[:, pb:pb + ICc], AF.Silu)))

            @block.vector
            def _(v):
                k.waits(v, (tag, "A.mmU", i), (tag, "A.g", i),
                        (tag, "A.V", i - 2))
                k.put((tag, "A.u", i), k.done("v.ffa", v.tensor_copy(
                    ubuf[:, sb:sb + ICc], pu[:, pb:pb + ICc])))
                v.tensor_tensor(hbuf[:, sb:sb + ICc], gbuf[:, sb:sb + ICc],
                                ubuf[:, sb:sb + ICc], ALU.mult)
                v.tensor_tensor(abuf[:, sb:sb + ICc], hbuf[:, sb:sb + ICc],
                                hbuf[:, sb:sb + ICc], ALU.mult)
                Q = ICc // 4
                a3 = abuf[:, sb:sb + ICc].rearrange("p (g f) -> p g f", f=4)
                tb = (i % 2) * ICc
                t0 = tbuf[:, tb + 0 * Q: tb + 1 * Q]
                t1 = tbuf[:, tb + 1 * Q: tb + 2 * Q]
                t2 = tbuf[:, tb + 2 * Q: tb + 3 * Q]
                t3 = tbuf[:, tb + 3 * Q: tb + 4 * Q]
                v.tensor_tensor(t0, a3[:, :, 0], a3[:, :, 1], ALU.max)
                v.tensor_tensor(t1, a3[:, :, 0], a3[:, :, 1], ALU.min)
                v.tensor_tensor(t2, a3[:, :, 2], a3[:, :, 3], ALU.max)
                v.tensor_tensor(t3, a3[:, :, 2], a3[:, :, 3], ALU.min)
                v.tensor_tensor(t0, t0, t2, ALU.min)
                v.tensor_tensor(t1, t1, t3, ALU.max)
                v.tensor_tensor(t0, t0, t1, ALU.max)
                v.tensor_tensor(a3, a3, t0[:, :, None].broadcast_to([128, Q, 4]),
                                ALU.is_ge)
                v.tensor_tensor(hbuf[:, sb:sb + ICc], hbuf[:, sb:sb + ICc],
                                abuf[:, sb:sb + ICc], ALU.mult)
                bslot = (ic % 2) * NTI * ICc
                k.wait(v, (tag, "A.band", ic - 2))
                k.put((tag, "A.V", i), k.done("v.ffa", v.tensor_scalar(
                    band[:, bslot + st * ICc: bslot + (st + 1) * ICc],
                    hbuf[:, sb:sb + ICc], 0.5, None, ALU.mult)))

        @block.sync
        def _(sync):
            k.wait(sync, (tag, "A.V", ic * NTI + NTI - 1))
            bslot = (ic % 2) * NTI * ICc
            k.put((tag, "A.band", ic), k.done(f"d.ffa{ic % 2}", sync.dma_start(
                out=wcol_ap(hs_dram, ic * ICc, ICc, 0, NTI),
                in_=band[:, bslot:bslot + NTI * ICc].rearrange(
                    "p (t c) -> p t c", c=ICc)), 16))

    k.put((f"{tag}:done",), k.tk[(tag, "A.band", NIC_ - 1)])


def emit_ffn_b(k, block, tag, wd, hs_dram, x_in, x_out, row0, NTI, ps,
               slab, wdb, xb, xo, pred=None):
    NG = NTI // 4
    NH = 2
    NB = 16
    wd4 = wdb.rearrange("p (s t c) -> p s t c", s=2, c=1024)

    for sg in range(NG):
        @block.scalar
        def _(s):
            if sg == 0:
                k.wait(s, pred)
            else:
                k.wait(s, (tag, "B.T", sg * NH - 1))
            tk = None
            for isx in range(64):
                tk = k.done("sd.ffb", s.dma_start(
                    out=slab[:, isx * 512:(isx + 1) * 512],
                    in_=hs_dram[sg * 512:(sg + 1) * 512, isx * 128:(isx + 1) * 128],
                    transpose=True), 16)
            k.put((tag, "B.slab", sg), tk)

        for hh in range(NH):
            r = sg * NH + hh

            for b in range(NB):
                n = r * NB + b

                @block.gpsimd
                def _(g):
                    if n == 0:
                        k.wait(g, pred)
                    k.wait(g, (tag, "B.Tb", n - 2))
                    k.put((tag, "B.w", n), k.done(f"g.ffb{n % 2}", g.dma_start(
                        out=wd4[:, n % 2],
                        in_=wcol_ap(wd, hh * 1024, 1024, 4 * b, 4)), 16))

                @block.tensor
                def _(t_):
                    if b == 0:
                        k.wait(t_, (tag, "B.slab", sg))
                        k.waits(t_, *[(tag, "B.fl", r - 1, st) for st in range(4)])
                    k.wait(t_, (tag, "B.w", n))
                    ins = None
                    for j in range(4):
                        isx = b * 4 + j
                        for st in range(4):
                            for nh in range(2):
                                ins = t_.matmul(
                                    ps[st][:, nh * 512:(nh + 1) * 512],
                                    slab[:, isx * 512 + st * 128: isx * 512 + (st + 1) * 128],
                                    wd4[:, n % 2, j, nh * 512:(nh + 1) * 512],
                                    start=(isx == 0), stop=(isx == 63))
                    k.put((tag, "B.Tb", n), k.done("t.ffb", ins))
                    if b == NB - 1:
                        k.put((tag, "B.T", r), k.tk[(tag, "B.Tb", n)])

            @block.sync
            def _(sync):
                if r == 0:
                    k.wait(sync, pred)
                k.waits(sync, *[(tag, "B.fl", r - 1, st) for st in range(4)])
                tk = None
                for st in range(4):
                    tk = k.done("d.ffb.x", sync.dma_start(
                        out=xb[:, st * 1024:(st + 1) * 1024],
                        in_=x_in[row0 + (sg * 4 + st) * 128: row0 + (sg * 4 + st + 1) * 128,
                                 hh * 1024:(hh + 1) * 1024]), 16)
                k.put((tag, "B.x", r), tk)

            @block.vector
            def _(v):
                k.waits(v, (tag, "B.T", r), (tag, "B.x", r),
                        *[(tag, "B.st", r - 1, st) for st in range(4)])
                for st in range(4):
                    k.put((tag, "B.fl", r, st), k.done("v.ffb", v.tensor_tensor(
                        xo[:, st * 1024:(st + 1) * 1024],
                        xb[:, st * 1024:(st + 1) * 1024],
                        ps[st][:, :], ALU.add)))

            @block.sync
            def _(sync):
                for st in range(4):
                    k.wait(sync, (tag, "B.fl", r, st))
                    k.put((tag, "B.st", r, st), k.done(f"d.ffb.s{st}", sync.dma_start(
                        out=x_out[row0 + (sg * 4 + st) * 128: row0 + (sg * 4 + st + 1) * 128,
                                  hh * 1024:(hh + 1) * 1024],
                        in_=xo[:, st * 1024:(st + 1) * 1024]), 16))

    k.put((f"{tag}:done",), k.tk[(tag, "B.st", NG * NH - 1, 3)])


def emit_qkv(k, block, tag, ytag, wqkv, bvec, qT_d, kT_d, v_d, yT, RSLAB,
             ident, ps, wvT_arena, scratch, pred=None):
    NST = RSLAB // 128
    wrow = scratch[:, 0:4096]
    wtile = scratch[:, 4096:8192]
    bvb = scratch[:, 8192:10240]
    otile = scratch[:, 10240:12288]
    brow = scratch[0:1, 12288:14336]
    bcols = scratch[:, 14336:14432].bitcast(F32)
    ones = scratch[0:1, 14464:14592]
    ptrb = ps[2][:, :].bitcast(BF16)
    pq = ps[0]
    pv = ps[1]

    @block.gpsimd
    def _(g):
        k.wait(g, pred)
        k.put((tag, "ones"), k.done("g.qkvs", g.memset(ones[:, :], 1.0)))

    for bq in range(3):
        @block.gpsimd
        def _(g):
            if bq == 0:
                k.wait(g, pred)
            else:
                k.wait(g, (tag, "btr", bq - 1))
            k.put((tag, "brow", bq), k.done("g.qkvb", g.dma_start(
                out=brow[:, :], in_=bvec[None, bq * 2048:(bq + 1) * 2048]), 16))

        @block.tensor
        def _(t_):
            if bq == 0:
                k.waits(t_, ("setup",), pred)
            else:
                k.wait(t_, (tag, "bce", bq - 1))
            k.wait(t_, (tag, "brow", bq))
            ins = None
            for j in range(16):
                ins = t_.matmul(ps[2][:, j:j + 1],
                                brow[:, j * 128:(j + 1) * 128],
                                ident[0:1, 0:1], start=(j == 0), stop=(j == 15))
            k.put((tag, "btr", bq), k.done("t.qkv", ins))

        @block.vector
        def _(v):
            k.wait(v, (tag, "btr", bq))
            k.put((tag, "bce", bq), k.done("v.qkv", v.tensor_copy(
                bcols[:, bq * 16:(bq + 1) * 16], ps[2][:, 0:16])))

    @block.tensor
    def _(t_):
        k.waits(t_, (tag, "ones"), (tag, "brow", 2), (tag, "bce", 2))
        for nh in range(2):
            ins = t_.matmul(pq[:, nh * 512:(nh + 1) * 512], ones[:, :],
                            brow[:, nh * 512:(nh + 1) * 512],
                            start=True, stop=True)
        k.put((tag, "bvb0"), k.done("t.qkv", ins))
        for nh in range(2):
            ins = t_.matmul(pv[:, nh * 512:(nh + 1) * 512], ones[:, :],
                            brow[:, 1024 + nh * 512: 1024 + (nh + 1) * 512],
                            start=True, stop=True)
        k.put((tag, "bvb1"), k.done("t.qkv", ins))

    @block.vector
    def _(v):
        k.wait(v, (tag, "bvb0"))
        k.done("v.qkv", v.tensor_copy(bvb[:, 0:1024], pq[:, :]))
        k.wait(v, (tag, "bvb1"))
        k.put((tag, "bvb"), k.done("v.qkv", v.tensor_copy(bvb[:, 1024:2048],
                                                          pv[:, :])))

    for ot in range(32):
        wslot = ot % 2

        @block.gpsimd
        def _(g):
            k.waits(g, (tag, "wtr", ot - 2),
                    ((tag, "bvb1") if ot == 0 else None))
            k.put((tag, "wrow", ot), k.done(f"g.qkv{wslot}", g.dma_start(
                out=wrow[:, wslot * H:(wslot + 1) * H],
                in_=wqkv[ot * 128:(ot + 1) * 128, :]), 16))

        @block.tensor
        def _(t_):
            k.waits(t_, (tag, "wrow", ot), (tag, "wte", ot - 1),
                    ((tag, "bce", 2) if ot == 0 else None))
            ins = None
            for ht in range(HT):
                ins = t_.transpose(ptrb[:, ht * 128:(ht + 1) * 128],
                                   wrow[:, wslot * H + ht * 128: wslot * H + (ht + 1) * 128],
                                   ident[:, :])
            k.put((tag, "wtr", ot), k.done("t.qkv", ins))

        @block.scalar
        def _(s):
            k.waits(s, (tag, "wtr", ot), (tag, "wtconsumed", ot - 2))
            k.put((tag, "wte", ot), k.done("s.qkv", s.copy(
                wtile[:, wslot * H:(wslot + 1) * H], ptrb[:, :])))

        for sc in range(4):
            @block.tensor
            def _(t_):
                k.waits(t_, (tag, "wte", ot),
                        *[(f"{ytag}:yT", 4 * sc + j) for j in range(4)],
                        (tag, "qkevu", (ot * 4 + sc) - 2),
                        ((tag, "bvb") if ot == 0 and sc == 0 else None))
                ins = None
                for ht in range(HT):
                    ins = t_.matmul(pq[:, (sc % 2) * 512:(sc % 2) * 512 + 512],
                                    wtile[:, wslot * H + ht * 128: wslot * H + (ht + 1) * 128],
                                    yT[:, ht * RSLAB + sc * 512: ht * RSLAB + (sc + 1) * 512],
                                    start=(ht == 0), stop=(ht == HT - 1))
                tk = k.done("t.qkv", ins)
                k.put((tag, "qkmm", ot, sc), tk)
                if sc == 3:
                    k.put((tag, "wtconsumed", ot), tk)

            @block.vector
            def _(v):
                u = ot * 4 + sc
                k.waits(v, (tag, "qkmm", ot, sc), (tag, "qkstu", u - 2))
                oslot = u % 2
                k.put((tag, "qkevu", u), k.done("v.qkv", v.tensor_scalar(
                    otile[:, oslot * 512:(oslot + 1) * 512],
                    pq[:, (sc % 2) * 512:(sc % 2) * 512 + 512],
                    bcols[:, ot:ot + 1], None, ALU.add)))

            @block.sync
            def _(sync):
                u = ot * 4 + sc
                k.wait(sync, (tag, "qkevu", u))
                oslot = u % 2
                if ot < 16:
                    out_ap = qT_d[ot * 128:(ot + 1) * 128, sc * 512:(sc + 1) * 512]
                    in_ap = otile[:, oslot * 512:(oslot + 1) * 512]
                else:
                    out_ap = kT_d[sc * 4:(sc + 1) * 4, ot - 16].transpose([1, 0, 2])
                    in_ap = otile[:, oslot * 512:(oslot + 1) * 512].rearrange(
                        "p (j c) -> p j c", c=128)
                k.put((tag, "qkstu", u), k.done(
                    f"d.qkv{oslot}", sync.dma_start(out=out_ap, in_=in_ap), 16))

    wvT = wvT_arena
    for oc in range(4):
        vslot = oc % 2
        for tt in range(4):
            m = oc * 4 + tt
            wsl = m % 2

            @block.gpsimd
            def _(g):
                if m < 2:
                    k.wait(g, (tag, "wtr", 30 + m))
                else:
                    k.wait(g, (tag, "vwtrm", m - 2))
                row0 = 4096 + oc * 512 + tt * 128
                k.put((tag, "vwrow", oc, tt), k.done(f"g.qkvv{wsl}", g.dma_start(
                    out=wrow[:, wsl * H:(wsl + 1) * H],
                    in_=wqkv[row0:row0 + 128, :]), 16))

            @block.tensor
            def _(t_):
                k.waits(t_, (tag, "vwrow", oc, tt), (tag, "vwte", oc, tt - 1),
                        ((tag, "vwte", oc - 1, 3) if tt == 0 else None),
                        ((tag, "wte", 31) if (oc == 0 and tt == 0) else None))
                ins = None
                for ht in range(HT):
                    ins = t_.transpose(ptrb[:, ht * 128:(ht + 1) * 128],
                                       wrow[:, wsl * H + ht * 128: wsl * H + (ht + 1) * 128],
                                       ident[:, :])
                tk = k.done("t.qkv", ins)
                k.put((tag, "vwtr", oc, tt), tk)
                k.put((tag, "vwtrm", m), tk)

            @block.scalar
            def _(s):
                k.waits(s, (tag, "vwtr", oc, tt), (tag, "vslabfree", oc - 2))
                k.put((tag, "vwte", oc, tt), k.done("s.qkv", s.copy(
                    wvT[:, vslot * 16 * 512:(vslot + 1) * 16 * 512]
                    .rearrange("p (h c) -> p h c", c=512)[:, :, tt * 128:(tt + 1) * 128],
                    ptrb[:, :].rearrange("p (h c) -> p h c", c=128))))

        for st in range(NST):
            @block.tensor
            def _(t_):
                u2 = oc * NST + st
                k.waits(t_, (tag, "vwte", oc, 3), (f"{ytag}:yT", st),
                        (tag, "vevu", u2 - 2),
                        ((tag, "bvb") if u2 == 0 else None))
                ins = None
                for ht in range(HT):
                    ins = t_.matmul(pv[:, (st % 2) * 512:(st % 2) * 512 + 512],
                                    yT[:, ht * RSLAB + st * 128: ht * RSLAB + (st + 1) * 128],
                                    wvT[:, vslot * 16 * 512 + ht * 512: vslot * 16 * 512 + (ht + 1) * 512],
                                    start=(ht == 0), stop=(ht == HT - 1))
                tk = k.done("t.qkv", ins)
                k.put((tag, "vmm", oc, st), tk)
                if st == NST - 1:
                    k.put((tag, "vslabfree", oc), tk)

            @block.vector
            def _(v):
                u2 = oc * NST + st
                k.waits(v, (tag, "vmm", oc, st), (tag, "bvb"),
                        (tag, "vstu", u2 - 2))
                oslot = 2 + u2 % 2
                k.put((tag, "vevu", u2), k.done("v.qkv", v.tensor_tensor(
                    otile[:, oslot * 512:(oslot + 1) * 512],
                    pv[:, (st % 2) * 512:(st % 2) * 512 + 512],
                    bvb[:, oc * 512:(oc + 1) * 512], ALU.add)))

            @block.sync
            def _(sync):
                u2 = oc * NST + st
                k.wait(sync, (tag, "vevu", u2))
                oslot = 2 + u2 % 2
                k.put((tag, "vstu", u2), k.done(
                    f"d.qkvv{u2 % 2 + 2}", sync.dma_start(
                        out=v_d[st * 128:(st + 1) * 128, oc * 512:(oc + 1) * 512],
                        in_=otile[:, oslot * 512:(oslot + 1) * 512]), 16))

    k.put((f"{tag}:done",), k.tk[(tag, "vstu", 4 * NST - 1)])


def emit_woutT(k, block, tag, wout, woutT_d, ident, ps, scratch, pred=None):
    wrow = scratch[:, 0:4096]
    wslab = scratch[:, 4096:6144]
    ptrb = ps[2][:, :].bitcast(BF16)
    for ot in range(16):
        wsl = ot % 2

        @block.gpsimd
        def _(g):
            if ot < 2:
                k.wait(g, pred)
            k.wait(g, (tag, "tr", ot - 2))
            k.put((tag, "w", ot), k.done(f"g.wo{wsl}", g.dma_start(
                out=wrow[:, wsl * H:(wsl + 1) * H],
                in_=wout[ot * 128:(ot + 1) * 128, :]), 16))

        @block.tensor
        def _(t_):
            if ot == 0:
                k.waits(t_, ("setup",), pred)
            k.waits(t_, (tag, "w", ot), (tag, "ev", ot - 1))
            ins = None
            for ht in range(HT):
                ins = t_.transpose(ptrb[:, ht * 128:(ht + 1) * 128],
                                   wrow[:, wsl * H + ht * 128: wsl * H + (ht + 1) * 128],
                                   ident[:, :])
            k.put((tag, "tr", ot), k.done("t.wo", ins))

        @block.scalar
        def _(s):
            k.waits(s, (tag, "tr", ot), (tag, "st", ot - 1))
            k.put((tag, "ev", ot), k.done("s.wo", s.copy(wslab[:, :], ptrb[:, :])))

        @block.sync
        def _(sync):
            k.wait(sync, (tag, "ev", ot))
            out_ap = woutT_d.rearrange("(t p) c -> t p c", p=128)[
                :, :, ot * 128:(ot + 1) * 128].transpose([1, 0, 2])
            k.put((tag, "st", ot), k.done("d.wo", sync.dma_start(
                out=out_ap,
                in_=wslab[:, :].rearrange("p (t c) -> p t c", c=128)), 16))
    k.put((f"{tag}:done",), k.tk[(tag, "st", 15)])


def emit_attn(k, block, tag, qT_d, kT_f, v_f, x1_d, woutT_d, bout, x2_d,
              RQ, NK, ident, identf32, consts, ps, arena0, warena, barena,
              pred=None):
    NCH = RQ // 512
    NKB = NK // 1024
    Vkb = arena0[:, 0:16384]
    Vkb2 = barena[:, 0:16384]
    accT = arena0[:, 16384:32768].bitcast(F32)
    qTc = warena[:, 0:8192]
    accTb = qTc
    PT = warena[:, 8192:16384]
    ksub = warena[:, 16384:20480]
    Ssb = warena[:, 20480:22528].bitcast(F32)
    m0b = warena[:, 22528:23552].bitcast(F32)
    woT = barena[:, 0:8192]
    x1t = barena[:, 8192:16384].bitcast(F32)
    stats = k.sbuf("fl.stats", [128, 16], F32)
    statsb = k.sbuf("fl.statsb", [128, 8], BF16)
    lrow = k.sbuf("fl.lrow", [1, 1024], F32)
    mrow = k.sbuf("fl.mrow", [1, 512], BF16)
    ones1 = k.sbuf("fl.ones1", [1, 128], BF16)
    onesc = k.sbuf("fl.onesc", [128, 1], BF16)
    bob = k.sbuf("fl.bob", [128, H], BF16)
    x2s = k.sbuf("fl.x2s", [128, 2 * 512], F32)
    ptrb = ps[2][:, :].bitcast(BF16)

    def vslab(kb):
        return Vkb if kb % 2 == 0 else Vkb2

    @block.gpsimd
    def _(g):
        k.wait(g, pred)
        k.done("g.fl", g.memset(ones1[:, :], 1.0))
        k.put((tag, "ones"), k.done("g.fl", g.memset(onesc[:, :], 1.0)))
        k.put((tag, "borow"), k.done("g.flb", g.dma_start(
            out=PT[0:1, 0:2048], in_=bout[None, :]), 16))

    @block.tensor
    def _(t_):
        k.waits(t_, ("setup",), (tag, "ones"), (tag, "borow"), pred)
        for nh in range(2):
            ins = t_.matmul(ps[0][:, nh * 512:(nh + 1) * 512],
                            ones1[:, :], PT[0:1, nh * 512:(nh + 1) * 512],
                            start=True, stop=True)
        k.put((tag, "bobm0"), k.done("t.fl", ins))
        for nh in range(2):
            ins = t_.matmul(ps[1][:, nh * 512:(nh + 1) * 512],
                            ones1[:, :], PT[0:1, 1024 + nh * 512: 1024 + (nh + 1) * 512],
                            start=True, stop=True)
        k.put((tag, "bobm1"), k.done("t.fl", ins))

    @block.vector
    def _(v):
        k.wait(v, (tag, "bobm0"))
        k.done("v.fl", v.tensor_copy(bob[:, 0:1024], ps[0][:, :]))
        k.wait(v, (tag, "bobm1"))
        k.put((tag, "bob"), k.done("v.fl", v.tensor_copy(bob[:, 1024:2048],
                                                         ps[1][:, :])))

    for qc in range(NCH):
        @block.sync
        def _(sync):
            if qc == 0:
                k.wait(sync, pred)
            k.wait(sync, (tag, "opmm", qc - 1, 15))
            tk = None
            for ht in range(HT):
                tk = k.done("d.fq", sync.dma_start(
                    out=qTc[:, ht * 512:(ht + 1) * 512],
                    in_=qT_d[ht * 128:(ht + 1) * 128, qc * 512:(qc + 1) * 512]), 16)
            k.put((tag, "qTc", qc), tk)

        for sub in range(8):
            j = sub

            @block.sync
            def _(sync):
                k.waits(sync, (tag, "kcons", qc, j - 2),
                        ((tag, "kcons", qc - 1, 8 * NKB + 6 + (j % 2)) if (qc > 0 and j < 2) else None))
                k.put((tag, "kld", qc, j), k.done(f"d.fk{j % 2}", sync.dma_start(
                    out=ksub[:, (j % 2) * 2048:(j % 2) * 2048 + 2048],
                    in_=kT_f[sub].transpose([1, 0, 2])), 16))

            @block.tensor
            def _(t_):
                k.waits(t_, (tag, "qTc", qc), (tag, "kld", qc, j),
                        ((tag, "lev", qc - 1) if sub == 0 else None),
                        (tag, "m0red", qc, sub - 1))
                ins = None
                for st in range(4):
                    for ht in range(HT):
                        ins = t_.matmul(
                            ps[3][:, st * 128:(st + 1) * 128],
                            qTc[:, ht * 512 + st * 128: ht * 512 + (st + 1) * 128],
                            ksub[:, (j % 2) * 2048 + ht * 128: (j % 2) * 2048 + (ht + 1) * 128],
                            start=(ht == 0 and st == 0), stop=(ht == HT - 1 and st == 3))
                tk = k.done("t.fl", ins)
                k.put((tag, "m0mm", qc, sub), tk)
                k.put((tag, "kcons", qc, j), tk)

            @block.vector
            def _(v):
                k.wait(v, (tag, "m0mm", qc, sub))
                chain(k, v, "v.fl", v.reduce_max(
                    stats[:, 8:12],
                    ps[3][:, 0:512].rearrange("p (s c) -> p s c", c=128), axis=AX))
                if sub == 0:
                    tk = k.done("v.fl", v.tensor_copy(stats[:, 0:4],
                                                      stats[:, 8:12]))
                else:
                    tk = k.done("v.fl", v.tensor_tensor(
                        stats[:, 0:4], stats[:, 0:4], stats[:, 8:12], ALU.max))
                k.put((tag, "m0red", qc, sub), tk)

        @block.vector
        def _(v):
            k.wait(v, (tag, "m0red", qc, 7))
            k.put((tag, "m0bf", qc), k.done("v.fl", v.tensor_copy(
                statsb[:, 0:4], stats[:, 0:4])))

        @block.tensor
        def _(t_):
            k.waits(t_, (tag, "m0bf", qc), (tag, "mrow", qc - 1),
                    (tag, "ctxdone", qc - 1, NKB - 1),
                    (pred if qc == 0 else None))
            ins = None
            for st in range(4):
                ins = t_.transpose(ptrb[0:1, st * 128:(st + 1) * 128],
                                   statsb[:, st:st + 1], ident[:, :])
            k.put((tag, "m0tr", qc), k.done("t.fl", ins))

        @block.vector
        def _(v):
            k.wait(v, (tag, "m0tr", qc))
            k.put((tag, "mrow", qc), k.done("v.fl", v.tensor_copy(
                mrow[0:1, :], ptrb[0:1, 0:512])))

        @block.tensor
        def _(t_):
            k.waits(t_, (tag, "mrow", qc), (tag, "m0bev", qc - 1),
                    ((tag, "bob") if qc == 0 else None),
                    (tag, "opsc", qc - 1, 14))
            k.put((tag, "m0bm", qc), k.done("t.fl", t_.matmul(
                ps[0][:, 0:512], ones1[:, :], mrow[0:1, :],
                start=True, stop=True)))

        @block.vector
        def _(v):
            k.waits(v, (tag, "m0bm", qc), ((tag, "bob") if qc == 0 else None))
            k.put((tag, "m0bev", qc), k.done("v.fl", v.tensor_copy(
                m0b[:, :], ps[0][:, 0:512])))

        for kb in range(NKB):
            kslot = kb % 2

            @block.sync
            def _(sync):
                k.waits(sync, (tag, "ctxdone", qc, kb - 2),
                        ((tag, "opst", qc - 1, 15) if (qc > 0 and kb <= 1) else None),
                        (pred if (qc == 0 and kb <= 1) else None))
                tk = None
                for sub in range(8):
                    tk = k.done(f"d.fv{kslot}", sync.dma_start(
                        out=vslab(kb)[:, sub * 2048:(sub + 1) * 2048],
                        in_=v_f[kb * 1024 + sub * 128: kb * 1024 + (sub + 1) * 128, :]), 16)
                k.put((tag, "vload", qc, kb), tk)

            for sub in range(8):
                e = kb * 8 + sub
                j = 8 + e

                @block.sync
                def _(sync):
                    k.wait(sync, (tag, "kcons", qc, j - 2))
                    k.put((tag, "kld", qc, j), k.done(f"d.fk{j % 2}", sync.dma_start(
                        out=ksub[:, (j % 2) * 2048:(j % 2) * 2048 + 2048],
                        in_=kT_f[e].transpose([1, 0, 2])), 16))

                @block.tensor
                def _(t_):
                    k.waits(t_, (tag, "kld", qc, j),
                            (tag, "ssub", qc, e - 2),
                            ((tag, "m0bev", qc) if e == 0 else None),
                            ((tag, "opsc", qc - 1, 15) if e == 1 else None),
                            ((tag, "bob") if (qc == 0 and e == 1) else None))
                    ins = None
                    for ht in range(HT):
                        ins = t_.matmul(
                            ps[0][:, (e % 2) * 512:(e % 2) * 512 + 512],
                            ksub[:, (j % 2) * 2048 + ht * 128: (j % 2) * 2048 + (ht + 1) * 128],
                            qTc[:, ht * 512:(ht + 1) * 512],
                            start=(ht == 0), stop=(ht == HT - 1))
                    tk = k.done("t.fl", ins)
                    k.put((tag, "scmm", qc, e), tk)
                    k.put((tag, "kcons", qc, j), tk)

                @block.vector
                def _(v):
                    k.waits(v, (tag, "scmm", qc, e), (tag, "exp", qc, e - 2))
                    k.put((tag, "ssub", qc, e), k.done("v.fl", v.tensor_tensor(
                        Ssb[:, (e % 2) * 512:(e % 2) * 512 + 512],
                        ps[0][:, (e % 2) * 512:(e % 2) * 512 + 512],
                        m0b[:, :], ALU.subtract)))

                @block.scalar
                def _(s):
                    k.waits(s, (tag, "ssub", qc, e),
                            (tag, "ctxdone", qc, kb - 2))
                    k.put((tag, "exp", qc, e), k.done("s.fl", s.activation(
                        PT[:, kslot * 4096 + sub * 512: kslot * 4096 + (sub + 1) * 512],
                        Ssb[:, (e % 2) * 512:(e % 2) * 512 + 512],
                        AF.Exp, bias=consts[:, 1:2], scale=ISCALE)))

            @block.tensor
            def _(t_):
                ins = None
                for sub in range(8):
                    e = kb * 8 + sub
                    k.wait(t_, (tag, "exp", qc, e))
                    ins = t_.matmul(
                        ps[3][0:1, 0:512], onesc[:, :],
                        PT[:, kslot * 4096 + sub * 512: kslot * 4096 + (sub + 1) * 512],
                        start=(kb == 0 and sub == 0),
                        stop=(kb == NKB - 1 and sub == 7))
                k.put((tag, "lmm", qc, kb), k.done("t.fl", ins))

            for ht in range(HT):
                @block.tensor
                def _(t_):
                    if ht == 0:
                        k.wait(t_, (tag, "vload", qc, kb))
                    if qc == 0 and kb == 0 and ht < 2:
                        k.wait(t_, (tag, "bob"))
                    if kb == 0 and ht == 2:
                        k.wait(t_, (tag, "mrow", qc))
                    if ht >= 4:
                        k.wait(t_, (tag, "ctxfl", qc, kb, ht - 4))
                    elif kb > 0:
                        k.wait(t_, (tag, "ctxfl", qc, kb - 1, ht + 12))
                    ins = None
                    for sub in range(8):
                        ins = t_.matmul(
                            (ps[1] if (ht % 4) < 2 else ps[2])[
                                :, (ht % 2) * 512:(ht % 2) * 512 + 512],
                            vslab(kb)[:, sub * 2048 + ht * 128: sub * 2048 + (ht + 1) * 128],
                            PT[:, kslot * 4096 + sub * 512: kslot * 4096 + (sub + 1) * 512],
                            start=(sub == 0), stop=(sub == 7))
                    k.put((tag, "ctxmm", qc, kb, ht), k.done("t.fl", ins))

                @block.vector
                def _(v):
                    k.wait(v, (tag, "ctxmm", qc, kb, ht))
                    pp = (ps[1] if (ht % 4) < 2 else ps[2])[
                        :, (ht % 2) * 512:(ht % 2) * 512 + 512]
                    if kb == 0:
                        tk = k.done("v.fl", v.tensor_copy(
                            accT[:, ht * 512:(ht + 1) * 512], pp))
                    else:
                        tk = k.done("v.fl", v.tensor_tensor(
                            accT[:, ht * 512:(ht + 1) * 512],
                            accT[:, ht * 512:(ht + 1) * 512], pp, ALU.add))
                    k.put((tag, "ctxfl", qc, kb, ht), tk)
                    if ht == HT - 1:
                        k.put((tag, "ctxdone", qc, kb), tk)

        @block.vector
        def _(v):
            k.wait(v, (tag, "lmm", qc, NKB - 1))
            chain(k, v, "v.fl", v.tensor_copy(lrow[0:1, 0:512],
                                              ps[3][0:1, 0:512]))
            tk = k.done("v.fl", v.reciprocal(lrow[0:1, 512:1024],
                                             lrow[0:1, 0:512]))
            k.put((tag, "lev", qc), tk)

        @block.tensor
        def _(t_):
            k.waits(t_, (tag, "lev", qc), (tag, "ctxdone", qc, NKB - 1),
                    (tag, "linv", qc - 1))
            ins = None
            for st in range(4):
                ins = t_.matmul(ps[3][:, 512 + st: 513 + st],
                                lrow[0:1, 512 + st * 128: 512 + (st + 1) * 128],
                                identf32[0:1, 0:1], start=(st == 0), stop=(st == 3))
            k.put((tag, "ltr", qc), k.done("t.fl", ins))

        @block.vector
        def _(v):
            k.wait(v, (tag, "ltr", qc))
            k.put((tag, "linv", qc), k.done("v.fl", v.tensor_copy(
                stats[:, 12:16], ps[3][:, 512:516])))

        @block.vector
        def _(v):
            k.waits(v, (tag, "ctxdone", qc, NKB - 1),
                    (tag, "kcons", qc, 8 + 8 * NKB - 1),
                    (tag, "opmm", qc - 1, 15))
            k.put((tag, "accTb", qc), k.done("v.fl", v.tensor_copy(
                accTb[:, :], accT[:, :])))

        for oc in range(4):
            @block.sync
            def _(sync):
                k.waits(sync, (tag, "ctxdone", qc, NKB - 1),
                        (tag, "opmm", qc, (oc - 2) * 4 + 3))
                tk = None
                for ht in range(HT):
                    tk = k.done(f"d.fw{oc % 2}", sync.dma_start(
                        out=woT[:, ht * 512:(ht + 1) * 512],
                        in_=woutT_d[ht * 128:(ht + 1) * 128, oc * 512:(oc + 1) * 512]), 16)
                k.put((tag, "woT", qc, oc), tk)

            for st in range(4):
                u = oc * 4 + st

                @block.sync
                def _(sync):
                    k.waits(sync, (tag, "opev", qc, u - 2),
                            ((tag, "ctxdone", qc, NKB - 1) if u < 2 else None))
                    k.put((tag, "x1t", qc, u), k.done(f"d.fx{u % 2}", sync.dma_start(
                        out=x1t[:, (u % 2) * 2048:(u % 2) * 2048 + 2048],
                        in_=x1_d[qc * 512 + st * 128: qc * 512 + (st + 1) * 128, :]), 16))

                @block.tensor
                def _(t_):
                    k.waits(t_, (tag, "accTb", qc), (tag, "woT", qc, oc),
                            (tag, "opsc", qc, u - 2),
                            ((tag, "ssub", qc, 8 * NKB - 2 + (u % 2)) if u < 2 else None))
                    ins = None
                    for ht in range(HT):
                        ins = t_.matmul(
                            ps[0][:, (u % 2) * 512:(u % 2) * 512 + 512],
                            accTb[:, ht * 512 + st * 128: ht * 512 + (st + 1) * 128],
                            woT[:, ht * 512:(ht + 1) * 512],
                            start=(ht == 0), stop=(ht == HT - 1))
                    k.put((tag, "opmm", qc, u), k.done("t.fl", ins))

                @block.scalar
                def _(s):
                    k.waits(s, (tag, "opmm", qc, u), (tag, "linv", qc),
                            (tag, "opst", qc, u - 2))
                    k.put((tag, "opsc", qc, u), k.done("s.fl", s.mul(
                        x2s[:, (u % 2) * 512:(u % 2) * 512 + 512],
                        ps[0][:, (u % 2) * 512:(u % 2) * 512 + 512],
                        stats[:, 12 + st:13 + st])))

                @block.vector
                def _(v):
                    k.waits(v, (tag, "opsc", qc, u), (tag, "x1t", qc, u),
                            (tag, "opst", qc, u - 2))
                    sl = (u % 2) * 512
                    chain(k, v, "v.fl", v.tensor_tensor(
                        x2s[:, sl:sl + 512], x2s[:, sl:sl + 512],
                        bob[:, oc * 512:(oc + 1) * 512], ALU.add))
                    tk = k.done("v.fl", v.tensor_tensor(
                        x2s[:, sl:sl + 512], x2s[:, sl:sl + 512],
                        x1t[:, (u % 2) * 2048 + oc * 512: (u % 2) * 2048 + (oc + 1) * 512],
                        ALU.add))
                    k.put((tag, "opev", qc, u), tk)
                    k.put((tag, "opadd", qc, u), tk)

                @block.sync
                def _(sync):
                    k.wait(sync, (tag, "opev", qc, u))
                    k.put((tag, "opst", qc, u), k.done(f"d.fo{u % 2}", sync.dma_start(
                        out=x2_d[qc * 512 + st * 128: qc * 512 + (st + 1) * 128,
                                 oc * 512:(oc + 1) * 512],
                        in_=x2s[:, (u % 2) * 512:(u % 2) * 512 + 512]), 16))

    k.put((f"{tag}:done",), k.tk[(tag, "opst", NCH - 1, 15)])
RS_ = 2048
NTI_ = RS_ // 128
NCORE = 8
SEQ = 16384


def build_launch1():
    nc = bass.Bass("TRN2", target_bir_lowering=False)
    x = nc.dram_tensor("x", [RS_, H], F32, kind="ExternalInput").ap()
    wg = nc.dram_tensor("wg", [H, I], F32, kind="ExternalInput").ap()
    wu = nc.dram_tensor("wu", [H, I], F32, kind="ExternalInput").ap()
    wd = nc.dram_tensor("wd", [I, H], F32, kind="ExternalInput").ap()
    wqkv = nc.dram_tensor("wqkv", [3 * H, H], F32, kind="ExternalInput").ap()
    bqkv = nc.dram_tensor("bqkv", [3 * H], F32, kind="ExternalInput").ap()
    x1 = nc.dram_tensor("x1", [RS_, H], F32, kind="ExternalOutput").ap()
    qT = nc.dram_tensor("qT", [H, RS_], BF16, kind="ExternalOutput").ap()
    kT = nc.dram_tensor("kT", [NTI_, HT, 128, 128], BF16, kind="ExternalOutput").ap()
    vv = nc.dram_tensor("v", [RS_, H], BF16, kind="ExternalOutput").ap()
    hs = nc.dram_tensor("hs", [RS_, I], BF16)

    with ExitStack() as ctx:
        k = K(nc, ctx)
        ident = k.sbuf("ident", [128, 128], BF16)
        consts = k.sbuf("consts", [128, 4], F32)
        arena0 = k.sbuf("arena0", [128, 32768], BF16)
        warena = k.sbuf("warena", [128, 3 * HT * 512], BF16)
        barena = k.sbuf("barena", [128, 16384], BF16)
        ps = [k.psum(f"ps{j}", [128, 1024], F32) for j in range(4)]
        block = ctx.enter_context(nc.Block())

        emit_setup(k, block, ident, consts)
        yT = arena0[:, 0:HT * RS_]
        emit_ln(k, block, "ln1", x, 0, NTI_, yT, RS_, ident, ps[3],
                consts[:, 0:1], pred=("setup",))
        emit_ffn_a(k, block, "f1a", "ln1", wg, wu, yT, RS_, hs, NTI_,
                   ps[0], ps[1], warena[:, :], barena[:, :])
        bar = barena[:, :].bitcast(F32)
        emit_ffn_b(k, block, "f1b", wd, hs, x, x1, 0, NTI_, ps,
                   slab=arena0[:, 0:64 * 512], wdb=warena[:, 0:8192],
                   xb=bar[:, 0:4096], xo=bar[:, 4096:8192],
                   pred=("f1a:done",))
        emit_ln(k, block, "ln2", x1, 0, NTI_, yT, RS_, ident, ps[3],
                consts[:, 0:1], pred=("f1b:done",))
        emit_qkv(k, block, "qkv", "ln2", wqkv, bqkv, qT, kT, vv, yT, RS_,
                 ident, ps, warena[:, 0:2 * 16 * 512], barena[:, :],
                 pred=("f1b:done",))

        @block.sync
        def _(sync):
            k.waits(sync,
                    *[("f1b", "B.st", r, st) for r in range(8) for st in range(4)],
                    ("qkv", "qkstu", 127), ("qkv", "qkstu", 126),
                    ("qkv", "vstu", 63), ("qkv", "vstu", 62),
                    ("qkv:done",))

    return nc


def build_launch2():
    nc = bass.Bass("TRN2", target_bir_lowering=False)
    x1 = nc.dram_tensor("x1", [RS_, H], F32, kind="ExternalInput").ap()
    qT = nc.dram_tensor("qT", [H, RS_], BF16, kind="ExternalInput").ap()
    kTf = nc.dram_tensor("kTf", [SEQ // 128, HT, 128, 128], BF16,
                         kind="ExternalInput").ap()
    vf = nc.dram_tensor("vf", [SEQ, H], BF16, kind="ExternalInput").ap()
    wout = nc.dram_tensor("wout", [H, H], F32, kind="ExternalInput").ap()
    bout = nc.dram_tensor("bout", [H], F32, kind="ExternalInput").ap()
    wg = nc.dram_tensor("wg", [H, I], F32, kind="ExternalInput").ap()
    wu = nc.dram_tensor("wu", [H, I], F32, kind="ExternalInput").ap()
    wd = nc.dram_tensor("wd", [I, H], F32, kind="ExternalInput").ap()
    o = nc.dram_tensor("o", [RS_, H], F32, kind="ExternalOutput").ap()
    woutT = nc.dram_tensor("woutT", [H, H], BF16)
    x2 = nc.dram_tensor("x2", [RS_, H], F32)
    hs = nc.dram_tensor("hs", [RS_, I], BF16)

    with ExitStack() as ctx:
        k = K(nc, ctx)
        ident = k.sbuf("ident", [128, 128], BF16)
        identf32 = k.sbuf("identf32", [128, 128], F32)
        consts = k.sbuf("consts", [128, 4], F32)
        arena0 = k.sbuf("arena0", [128, 32768], BF16)
        warena = k.sbuf("warena", [128, 3 * HT * 512], BF16)
        barena = k.sbuf("barena", [128, 16384], BF16)
        ps = [k.psum(f"ps{j}", [128, 1024], F32) for j in range(4)]
        block = ctx.enter_context(nc.Block())

        emit_setup(k, block, ident, consts)

        @block.gpsimd
        def _(g):
            tk = k.done("setup", g.memset(identf32[:, :], 0.0))
            g.wait_ge(tk[0], tk[1])
            ins = g.affine_select(out=identf32[:, :], in_=identf32[:, :],
                                  compare_op=ALU.not_equal, fill=1.0, base=0,
                                  pattern=[[-1, 128]], channel_multiplier=1)
            k.put(("setupf",), k.done("setup", ins))

        emit_woutT(k, block, "wo", wout, woutT, ident, ps,
                   barena[:, 0:8192], pred=("setupf",))
        emit_attn(k, block, "at", qT, kTf, vf, x1, woutT, bout, x2,
                  RS_, SEQ, ident, identf32, consts, ps, arena0, warena,
                  barena, pred=("wo:done",))
        yT = arena0[:, 0:HT * RS_]
        emit_ln(k, block, "ln3", x2, 0, NTI_, yT, RS_, ident, ps[3],
                consts[:, 0:1], pred=("at:done",))
        emit_ffn_a(k, block, "f3a", "ln3", wg, wu, yT, RS_, hs, NTI_,
                   ps[0], ps[1], warena[:, :], barena[:, :],
                   pred=("at:done",))
        bar = barena[:, :].bitcast(F32)
        emit_ffn_b(k, block, "f3b", wd, hs, x2, o, 0, NTI_, ps,
                   slab=arena0[:, 0:64 * 512], wdb=warena[:, 0:8192],
                   xb=bar[:, 0:4096], xo=bar[:, 4096:8192],
                   pred=("f3a:done",))

        @block.sync
        def _(sync):
            k.waits(sync,
                    *[("f3b", "B.st", r, st) for r in range(8) for st in range(4)])

    return nc


_CACHE = {}


def kernel(L0_attn_hidden_states, L0_attn_w_qkv, L0_attn_b_qkv,
           L0_attn_w_out, L0_attn_b_out, L0_ffn_w_gate, L0_ffn_w_up,
           L0_ffn_w_down):
    x_full = np.ascontiguousarray(
        np.asarray(L0_attn_hidden_states, dtype=np.float32).reshape(SEQ, H))
    wg = np.ascontiguousarray(np.asarray(L0_ffn_w_gate, dtype=np.float32))
    wu = np.ascontiguousarray(np.asarray(L0_ffn_w_up, dtype=np.float32))
    wd = np.ascontiguousarray(np.asarray(L0_ffn_w_down, dtype=np.float32))
    wqkv = np.ascontiguousarray(np.asarray(L0_attn_w_qkv, dtype=np.float32))
    bqkv = np.ascontiguousarray(np.asarray(L0_attn_b_qkv, dtype=np.float32))
    wout = np.ascontiguousarray(np.asarray(L0_attn_w_out, dtype=np.float32))
    bout = np.ascontiguousarray(np.asarray(L0_attn_b_out, dtype=np.float32))

    if "l1" not in _CACHE:
        _CACHE["l1"] = build_launch1()
    nc1 = _CACHE["l1"]
    in1 = [{"x": x_full[c * RS_:(c + 1) * RS_], "wg": wg, "wu": wu, "wd": wd,
            "wqkv": wqkv, "bqkv": bqkv} for c in range(NCORE)]
    r1 = run_bass_kernel_spmd(nc1, in1, core_ids=list(range(NCORE)))

    kTf = np.concatenate([r1.results[c]["kT"] for c in range(NCORE)], axis=0)
    vf = np.concatenate([r1.results[c]["v"] for c in range(NCORE)], axis=0)

    if "l2" not in _CACHE:
        _CACHE["l2"] = build_launch2()
    nc2 = _CACHE["l2"]
    in2 = [{"x1": r1.results[c]["x1"], "qT": r1.results[c]["qT"],
            "kTf": kTf, "vf": vf, "wout": wout, "bout": bout,
            "wg": wg, "wu": wu, "wd": wd} for c in range(NCORE)]
    r2 = run_bass_kernel_spmd(nc2, in2, core_ids=list(range(NCORE)))

    out = np.concatenate([r2.results[c]["o"] for c in range(NCORE)], axis=0)
    return out.reshape(1, SEQ, H)
